# version 01
# baseline (speedup 1.0000x reference)
import contextlib
import numpy as np
import concourse.bass as bass
import concourse.mybir as mybir
from concourse.bass_utils import run_bass_kernel_spmd

F32 = mybir.dt.float32
BF16 = mybir.dt.bfloat16
FP8 = mybir.dt.float8e4
ALU = mybir.AluOpType
ACT = mybir.ActivationFunctionType
AXX = mybir.AxisListType.X

SEQ, D, H, DK, INNER = 2048, 1024, 16, 64, 4096
NT = SEQ // 128
CHAIN = ("attn_a", "ffn", "attn_b", "ffn")


def ts(i, n):
    return slice(i * n, (i + 1) * n)


class Prog:

    ENGS = ("tensor", "vector", "scalar", "gpsimd", "sync")

    def __init__(self, nc):
        self.nc = nc
        self.ops = []
        self.hs = {}
        self.cnt = {}

    def _sem(self, name):
        if name not in self.hs:
            self.hs[name] = self.nc.alloc_semaphore(name)
            self.cnt[name] = 0
        return self.hs[name]

    def add(self, eng, fn, deps=(), inc=None, inc_k=1, pw=(), pr=()):
        waits = {}
        for d in deps:
            if d is None:
                continue
            s, v = d
            if waits.get(s, -1) < v:
                waits[s] = v
        ev = None
        if inc is not None:
            self._sem(inc)
            self.cnt[inc] += inc_k
            ev = (inc, self.cnt[inc])
        self.ops.append((eng, fn, waits, (inc, inc_k) if inc else None,
                         tuple(pw), tuple(pr)))
        return ev

    def dma(self, eng, out, in_, deps=(), q=None):
        q = q or ("dms" if eng == "sync" else "dmg")
        return self.add(eng, lambda e, o=out, i=in_: e.dma_start(out=o, in_=i),
                        deps=deps, inc=q, inc_k=16)

    def dmat(self, out, in_, deps=()):
        return self.add("sync", lambda e, o=out, i=in_: e.dma_start_transpose(o, i),
                        deps=deps, inc="dms", inc_k=16)

    ENG_SEM = {"tensor": "pe", "vector": "dv", "scalar": "ac", "gpsimd": "gp"}

    def barrier(self):
        for e in self.ENGS:
            deps = []
            own = self.ENG_SEM.get(e)
            if own and own in self.cnt:
                deps.append((own, self.cnt[own]))
            if e == "gpsimd" and "dmg" in self.cnt:
                deps.append(("dmg", self.cnt["dmg"]))
            if e == "sync" and "dms" in self.cnt:
                deps.append(("dms", self.cnt["dms"]))
            self.add(e, lambda en: en.nop(), deps=deps, inc="bar")
        return ("bar", self.cnt["bar"])

    def check_psum(self):
        engs = list(self.ENGS)
        eidx = {e: i for i, e in enumerate(engs)}
        incs = {}
        opinfo = []
        per_eng_count = {e: 0 for e in engs}
        for gi, (e, fn, waits, inc, pw, pr) in enumerate(self.ops):
            per_eng_count[e] += 1
            opinfo.append((gi, e, per_eng_count[e] - 1, waits, inc, pw, pr))
            if inc:
                incs.setdefault(inc[0], []).append((gi, None, e))
        cums = {}
        for s, lst in incs.items():
            c = 0
            for j, (gi, _, e) in enumerate(lst):
                k = self.ops[gi][3][1]
                c += k
                lst[j] = (gi, c, e)
        clocks = [None] * len(self.ops)
        eng_last = {e: None for e in engs}
        multi = {s for s, lst in incs.items()
                 if len({e for (_, _, e) in lst}) > 1}
        import bisect
        def sem_pred(s, v):
            lst = incs.get(s, [])
            res = []
            if s in multi:
                per = {}
                for (gi, c, e) in lst:
                    per.setdefault(e, []).append(gi)
                nper = v // len(engs)
                for e, gis in per.items():
                    if nper >= 1 and len(gis) >= nper:
                        res.append(gis[nper - 1])
                return res
            vals = [c for (_, c, _) in lst]
            j = bisect.bisect_left(vals, v)
            if j < len(lst):
                return [lst[j][0]]
            return [lst[-1][0]] if lst else []
        for gi, e, ei, waits, inc, pw, pr in opinfo:
            cl = [(-1)] * len(engs)
            if eng_last[e] is not None:
                cl = list(clocks[eng_last[e]])
            for s, v in waits.items():
                for p in sem_pred(s, v):
                    pc = clocks[p]
                    for k in range(len(engs)):
                        if pc[k] > cl[k]:
                            cl[k] = pc[k]
            cl[eidx[e]] = gi
            clocks[gi] = cl
            eng_last[e] = gi
        last_access = {}
        bad = []
        for gi, e, ei, waits, inc, pw, pr in opinfo:
            for b, rw in [(b, "w") for b in pw] + [(b, "r") for b in pr]:
                for (pgi, pe, prw) in last_access.get(b, []):
                    if pe == e:
                        continue
                    if prw == "r" and rw == "r":
                        continue
                    if clocks[gi][eidx[pe]] < pgi:
                        bad.append((b, pgi, pe, prw, gi, e, rw))
                la = [x for x in last_access.get(b, []) if clocks[gi][eidx[x[1]]] < x[0] or x[1] == e]
                la.append((gi, e, rw))
                last_access[b] = la[-6:]
        return bad

    def emit(self, block):
        for eng_name in self.ENGS:
            ops = [(f, w, i) for (e, f, w, i, pw, pr) in self.ops if e == eng_name]

            def section(engine, ops=ops, prog=self):
                hw = {}
                for fn, waits, inc in ops:
                    for s, v in waits.items():
                        if hw.get(s, -1) < v:
                            engine.wait_ge(prog.hs[s], v)
                            hw[s] = v
                    ins = fn(engine)
                    if inc is not None:
                        ins.then_inc(prog.hs[inc[0]], inc[1])
            getattr(block, eng_name)(section)


def build_program(sublayers=CHAIN, nseq=2):
    nc = bass.Bass("TRN2", target_bir_lowering=False)
    P = Prog(nc)

    x_in = nc.dram_tensor("x", [2, SEQ, D], F32, kind="ExternalInput").ap()
    y_out = nc.dram_tensor("y", [2, SEQ, D], F32, kind="ExternalOutput").ap()
    aW = {k: nc.dram_tensor("aW" + k, [D, D], F32, kind="ExternalInput").ap()
          for k in "qkvo"}
    bW = {k: nc.dram_tensor("bW" + k, [D, D], F32, kind="ExternalInput").ap()
          for k in "qkvo"}
    abq_in = nc.dram_tensor("abq", [128, 8], F32, kind="ExternalInput").ap()
    abk_in = nc.dram_tensor("abk", [128, 8], F32, kind="ExternalInput").ap()
    abo_in = nc.dram_tensor("abo", [1, D], F32, kind="ExternalInput").ap()
    abv_in = nc.dram_tensor("abv", [1, D], F32, kind="ExternalInput").ap()
    btrel_in = nc.dram_tensor("btrel", [H, 128, 1024], F32, kind="ExternalInput").ap()
    cbrel_in = nc.dram_tensor("cbrel", [128, H], F32, kind="ExternalInput").ap()
    masks_in = nc.dram_tensor("masks", [128, 256], F32, kind="ExternalInput").ap()
    wgT_in = nc.dram_tensor("wgT", [D, INNER], F32, kind="ExternalInput").ap()
    wuT_in = nc.dram_tensor("wuT", [D, INNER], F32, kind="ExternalInput").ap()
    wdT_in = nc.dram_tensor("wdT", [INNER, D], F32, kind="ExternalInput").ap()

    tg_d = nc.dram_tensor("tg_d", [128, 8 * INNER], FP8, kind="Internal").ap()
    tu_d = nc.dram_tensor("tu_d", [128, 8 * INNER], FP8, kind="Internal").ap()
    td_d = nc.dram_tensor("td_d", [128, 32 * D], FP8, kind="Internal").ap()

    stack = contextlib.ExitStack()
    sb = lambda n, s, d: stack.enter_context(nc.sbuf_tensor(n, s, d))
    pst = lambda n, s: stack.enter_context(nc.psum_tensor(n, s, F32))

    X = sb("X", [128, NT * D], F32)
    CONST = sb("CONST", [128, 8], F32)
    STAT = sb("STAT", [128, 128], F32)
    RS = sb("RS", [128, 128], F32)
    MW = sb("MW", [128, 8], F32)
    ONES = sb("ONES", [128, 128], BF16)
    ONESF = sb("ONESF", [128, 128], F32)
    MASKS = sb("MASKS", [128, 256], F32)
    CBREL = sb("CBREL", [128, H], F32)
    BQ = sb("BQ", [128, 8], F32)
    BK = sb("BK", [128, 8], F32)
    BO = sb("BO", [1, D], BF16)
    BV = sb("BV", [1, D], BF16)

    psS = [pst("psS0", [128, 1024]), pst("psS1", [128, 1024])]
    psPV = pst("psPV", [128, 512])
    psOP = pst("psOP", [128, 1024])
    psQ = pst("psQ", [128, 512])

    CAP = CONST[:, 6:7]

    last = {"dv": None, "ac": None, "gp": None}

    def chain(which, eng, fn, deps=(), inc=None):
        d = list(deps)
        if last[which] is not None:
            d.append(last[which])
        ev = P.add(eng, fn, deps=d, inc=inc or which)
        last[which] = ev
        return ev

    cdv = lambda fn, deps=(): chain("dv", "vector", fn, deps)
    cac = lambda fn, deps=(): chain("ac", "scalar", fn, deps)
    cgp = lambda fn, deps=(): chain("gp", "gpsimd", fn, deps)

    c = CONST
    cdv(lambda e: e.memset(c[:], 0.0))
    cdv(lambda e: e.memset(c[:, 0:1], 1.0 + 2.0 ** -23))
    cdv(lambda e: e.tensor_mul(c[:, 1:2], c[:, 0:1], c[:, 0:1]))
    cdv(lambda e: e.tensor_scalar(c[:, 2:3], c[:, 1:2], -1.0, -(2.0 ** -22), ALU.add, ALU.add))
    cdv(lambda e: e.tensor_scalar(c[:, 3:4], c[:, 2:3], 0.0, 2.0 ** -23, ALU.is_equal, ALU.mult))
    cdv(lambda e: e.scalar_tensor_tensor(c[:, 4:5], c[:, 2:3], 2.0 ** -6, c[:, 3:4], ALU.mult, ALU.add))
    cdv(lambda e: e.reciprocal(c[:, 5:6], c[:, 4:5]))
    cdv(lambda e: e.tensor_single_scalar(c[:, 6:7], c[:, 5:6], 1.5, ALU.mult))
    cdv(lambda e: e.memset(ONES[:], 1.0))
    cdv(lambda e: e.memset(ONESF[:], 1.0))
    cdv(lambda e: e.memset(STAT[:], 0.0))
    cdv(lambda e: e.memset(RS[:], 0.0))
    cdv(lambda e: e.memset(MW[:], 0.0))
    P.dma("gpsimd", BQ[:], abq_in)
    P.dma("gpsimd", BK[:], abk_in)
    P.dma("gpsimd", BO[:], abo_in)
    P.dma("gpsimd", BV[:], abv_in)
    P.dma("gpsimd", MASKS[:], masks_in)
    P.dma("gpsimd", CBREL[:], cbrel_in)

    bar = P.barrier()

    with contextlib.ExitStack() as wsk:
        wb = [wsk.enter_context(nc.sbuf_tensor("WQB0", [128, INNER], F32)),
              wsk.enter_context(nc.sbuf_tensor("WQB1", [128, INNER], F32))]
        wq2 = wsk.enter_context(nc.sbuf_tensor("WQ2", [128, INNER], F32))
        tqb = [wsk.enter_context(nc.sbuf_tensor("TQB0", [128, INNER], FP8)),
               wsk.enter_context(nc.sbuf_tensor("TQB1", [128, INNER], FP8))]

        def wquant(w_dram, t_dram, n, width, mw_col, gate_order, start_dep):
            loads, reds = [], []
            for i in range(n):
                dep = [start_dep] + ([reds[i - 2]] if i >= 2 else [])
                loads.append(P.dma("gpsimd", wb[i % 2][:, 0:width],
                                   w_dram[ts(i, 128), :], deps=dep))
                reds.append(P.add("vector",
                    lambda e, i=i: e.reduce_sum(STAT[:, i:i + 1], wb[i % 2][:, 0:width],
                                                axis=AXX, apply_absolute_value=True),
                    deps=[loads[i], last["dv"]], inc="dv"))
                last["dv"] = reds[-1]
            cdv(lambda e: e.reduce_sum(STAT[:, 32:33], STAT[:, 0:n], axis=AXX))
            e1 = P.add("tensor",
                       lambda e: e.matmul(psQ[0:1, 0:1], STAT[:, 32:33], ONESF[:, 0:1],
                                          start=True, stop=True),
                       deps=[last["dv"]], inc="pe", pw=(7,))
            chain("dv", "vector",
                  lambda e: e.tensor_scalar(STAT[0:1, 33:34], psQ[0:1, 0:1],
                                            1.0 / (n * 128 * width), 1e-5, ALU.mult, ALU.max),
                  deps=[e1])
            P.ops[-1] = P.ops[-1][:5] + ((7,),)
            e2 = P.add("tensor",
                       lambda e: e.matmul(psQ[0:128, 1:2], ONESF[0:1, 0:128],
                                          STAT[0:1, 33:34], start=True, stop=True),
                       deps=[last["dv"]], inc="pe", pw=(7,))
            chain("dv", "vector",
                  lambda e: e.tensor_copy(MW[:, mw_col:mw_col + 1], psQ[0:128, 1:2]), deps=[e2])
            P.ops[-1] = P.ops[-1][:5] + ((7,),)
            cdv(lambda e: e.tensor_single_scalar(STAT[:, 34:35], MW[:, mw_col:mw_col + 1], 0.5, ALU.mult))
            ev_th = cdv(lambda e: e.tensor_single_scalar(STAT[:, 35:36], MW[:, mw_col:mw_col + 1], -0.5, ALU.mult))

            st = []
            g2 = None
            for i in range(n):
                dep = [ev_th] + ([st[i - 2]] if i >= 2 else [])
                lv = P.dma("gpsimd", wb[i % 2][:, 0:width], w_dram[ts(i, 128), :], deps=dep)
                g1 = chain("dv", "vector",
                           lambda e, i=i: e.tensor_single_scalar(wq2[:, 0:width], wb[i % 2][:, 0:width],
                                                                 STAT[:, 35:36], ALU.is_le),
                           deps=[lv, (st[i - 2] if i >= 2 else None)])
                g2 = chain("dv", "vector",
                           lambda e, i=i: e.scalar_tensor_tensor(tqb[i % 2][:, 0:width], wb[i % 2][:, 0:width],
                                                                 STAT[:, 34:35], wq2[:, 0:width],
                                                                 ALU.is_ge, ALU.subtract),
                           deps=[g1])
                if gate_order:
                    o3 = t_dram.rearrange("p (o i f) -> p o i f", o=8, f=512)[:, :, i, :]
                    i3 = tqb[i % 2][:, 0:width].rearrange("p (o f) -> p o f", f=512)
                    ev = P.dma("sync", o3, i3, deps=[g2])
                else:
                    ev = P.dma("sync", t_dram[:, ts(i, width)], tqb[i % 2][:, 0:width], deps=[g2])
                st.append(ev)
            return st[-1], g2

        dep = bar
        for args in ((wgT_in, tg_d, 8, INNER, 0, True),
                     (wuT_in, tu_d, 8, INNER, 1, True),
                     (wdT_in, td_d, 32, D, 2, False)):
            ev_st, ev_g2 = wquant(*args, start_dep=dep)
            P.add("gpsimd", lambda e: e.nop(), deps=[ev_st, ev_g2], inc="gp")
            dep = ("gp", P.cnt["gp"])
            last["gp"] = dep
        bar = P.barrier()

    C_SUM, C_S2, C_MX, C_MN, C_M, C_H, C_R, C_T = 0, 16, 32, 48, 64, 80, 112, 96

    def stats_phase(bar, need_minmax, dump):
        evs = []
        for t in range(NT):
            e1 = P.add("vector",
                       lambda e, t=t, dump=dump: e.tensor_scalar(dump, X[:, ts(t, D)], 1.0, 0.0,
                                                      ALU.mult, ALU.add,
                                                      accum_out=STAT[:, C_SUM + t:C_SUM + t + 1]),
                       deps=[bar, last["dv"]], inc="dv")
            last["dv"] = e1
            e2 = P.add("vector",
                       lambda e, t=t, dump=dump: e.scalar_tensor_tensor(dump, X[:, ts(t, D)], 1.0,
                                                             X[:, ts(t, D)], ALU.mult, ALU.mult,
                                                             accum_out=STAT[:, C_S2 + t:C_S2 + t + 1]),
                       deps=[bar, e1], inc="dv")
            last["dv"] = e2
            evs.append(e2)
            if need_minmax:
                g1 = P.add("vector",
                           lambda e, t=t: e.reduce_max(STAT[:, C_MX + t:C_MX + t + 1], X[:, ts(t, D)], axis=AXX),
                           deps=[bar, e2], inc="dv")
                last["dv"] = g1
                g2 = P.add("vector",
                           lambda e, t=t: e.tensor_reduce(STAT[:, C_MN + t:C_MN + t + 1], X[:, ts(t, D)],
                                                          AXX, ALU.min),
                           deps=[bar, g1], inc="dv")
                last["dv"] = g2
                evs.append(g2)
        cdv(lambda e: e.tensor_single_scalar(STAT[:, C_M:C_M + 16], STAT[:, C_SUM:C_SUM + 16],
                                             1.0 / D, ALU.mult), deps=evs)
        cdv(lambda e: e.tensor_mul(STAT[:, C_T:C_T + 16], STAT[:, C_M:C_M + 16], STAT[:, C_M:C_M + 16]))
        cdv(lambda e: e.scalar_tensor_tensor(STAT[:, C_T:C_T + 16], STAT[:, C_S2:C_S2 + 16], 1.0 / D,
                                             STAT[:, C_T:C_T + 16], ALU.mult, ALU.subtract))
        ev_vp = cdv(lambda e: e.tensor_single_scalar(STAT[:, C_T:C_T + 16], STAT[:, C_T:C_T + 16],
                                                     1e-5, ALU.add))
        e_ln = cac(lambda e: e.activation(RS[:, 64:80], STAT[:, C_T:C_T + 16], ACT.Ln), deps=[ev_vp])
        e_r0 = cac(lambda e: e.activation(STAT[:, C_R:C_R + 16], RS[:, 64:80], ACT.Exp, scale=-0.5))
        cdv(lambda e: e.tensor_mul(RS[:, 64:80], STAT[:, C_R:C_R + 16], STAT[:, C_R:C_R + 16]), deps=[e_r0])
        cdv(lambda e: e.tensor_mul(RS[:, 64:80], RS[:, 64:80], STAT[:, C_T:C_T + 16]))
        cdv(lambda e: e.tensor_scalar(RS[:, 64:80], RS[:, 64:80], -0.5, 1.5, ALU.mult, ALU.add))
        cdv(lambda e: e.tensor_mul(STAT[:, C_R:C_R + 16], STAT[:, C_R:C_R + 16], RS[:, 64:80]))
        cdv(lambda e: e.scalar_tensor_tensor(STAT[:, C_H:C_H + 16], STAT[:, C_M:C_M + 16], -1.0,
                                             STAT[:, C_R:C_R + 16], ALU.mult, ALU.mult))
        return last["dv"]

    def attn_phase(bar, kind, seq, uid=[0]):
        a_mode = kind == "attn_a"
        Wsrc = aW if a_mode else bW
        uid[0] += 1
        pfx = "A%d_" % uid[0]
        with contextlib.ExitStack() as ask:
            asb = lambda n, s, d: ask.enter_context(nc.sbuf_tensor(pfx + n, s, d))
            WA = asb("WA", [128, 8 * D], BF16)
            WB = asb("WB", [128, 8 * D], BF16)
            XN = asb("XN", [128, 2 * D], BF16)
            XNT = asb("XNT", [128, 2 * D], BF16)
            KT = asb("KT", [128, 8 * SEQ], BF16)
            VT = asb("VT", [128, NT * D], BF16)
            QT = asb("QT", [128, 8 * 256], BF16)
            PB = [asb("P0", [128, SEQ], BF16), asb("P1", [128, SEQ], BF16)]
            PT = asb("PT", [128, SEQ], BF16)
            OTF = asb("OTF", [128, D], BF16)
            OTM = asb("OTM", [128, D], BF16)
            OT2 = asb("OT2", [128, D], BF16)
            DREL = asb("DREL", [128, 1024], F32)

            ev_stats = stats_phase(bar, need_minmax=False, dump=KT[:, 0:1024])

            w3 = lambda w: w.rearrange("(c p) o -> p c o", p=128)
            wsl = lambda t: t[:].rearrange("p (c o) -> p c o", o=D)
            ev_wk = P.dma("gpsimd", wsl(WA), w3(Wsrc["k"]), deps=[bar])
            ev_wv = P.dma("gpsimd", wsl(WB), w3(Wsrc["v"]), deps=[bar])

            xn_evs = [None] * NT
            kv_mm_evs = []
            xbar_prev = None
            kcons_prev = {}
            psS_free = {0: None, 1: None}
            kidx = 0
            for g in range(8):
                exn = []
                for ti in range(2):
                    t = 2 * g + ti
                    ev = P.add("vector",
                               lambda e, t=t, ti=ti: e.tensor_scalar(
                                   XN[:, ts(ti, D)], X[:, ts(t, D)],
                                   STAT[:, C_R + t:C_R + t + 1], STAT[:, C_H + t:C_H + t + 1],
                                   ALU.mult, ALU.add),
                               deps=[ev_stats, last["dv"], xbar_prev], inc="dv")
                    last["dv"] = ev
                    exn.append(ev)
                o3 = XNT[:].rearrange("p (c x) -> p c x", x=256)
                xb = []
                for ti in range(2):
                    i3 = XN[:, ts(ti, D)].rearrange("p (c f) -> p c f", f=128)
                    xb.append(P.dmat(o3[:, :, ts(ti, 128)], i3,
                                     deps=exn + [kcons_prev.get(g - 1)]))
                xbar_prev = xb[-1]

                for oc in range(8):
                    pp = kidx % 2
                    kidx += 1
                    pk = psS[pp]

                    def kmm(e, oc=oc, pk=pk):
                        ins = None
                        for ic in range(8):
                            ins = e.matmul(pk[:, 0:256],
                                           WA[:, ic * D + oc * 128: ic * D + (oc + 1) * 128],
                                           XNT[:, ts(ic, 256)],
                                           start=(ic == 0), stop=(ic == 7))
                        return ins
                    mm = P.add("tensor", kmm, deps=xb + [ev_wk, psS_free[pp]], inc="pe",
                               pw=(2 * pp,))
                    kv_mm_evs.append(mm)

                    if a_mode:
                        def kev(e, oc=oc, g=g, pk=pk):
                            return e.tensor_scalar(KT[:, oc * SEQ + g * 256: oc * SEQ + (g + 1) * 256],
                                                   pk[:, 0:256], BK[:, oc:oc + 1], None, ALU.add)
                    else:
                        def kev(e, oc=oc, g=g, pk=pk):
                            return e.tensor_copy(KT[:, oc * SEQ + g * 256: oc * SEQ + (g + 1) * 256],
                                                 pk[:, 0:256])
                    ev = P.add("vector", kev, deps=[mm, last["dv"]], inc="dv",
                               pr=(2 * pp,))
                    last["dv"] = ev
                    psS_free[pp] = ev
                for ti in range(2):
                    t = 2 * g + ti
                    pp = ti
                    pv = psS[pp]

                    def vmm(e, ti=ti, pv=pv, a_mode=a_mode):
                        laststop = not a_mode
                        ins = None
                        for ic in range(8):
                            st_ = laststop and (ic == 7)
                            e.matmul(pv[:, 0:512],
                                     XNT[:, ic * 256 + ti * 128: ic * 256 + (ti + 1) * 128],
                                     WB[:, ic * D: ic * D + 512],
                                     start=(ic == 0), stop=st_)
                            ins = e.matmul(pv[:, 512:1024],
                                           XNT[:, ic * 256 + ti * 128: ic * 256 + (ti + 1) * 128],
                                           WB[:, ic * D + 512: (ic + 1) * D],
                                           start=(ic == 0), stop=st_)
                        if a_mode:
                            e.matmul(pv[:, 0:512], ONES[0:1, 0:128], BV[0:1, 0:512],
                                     start=False, stop=True)
                            ins = e.matmul(pv[:, 512:1024], ONES[0:1, 0:128], BV[0:1, 512:1024],
                                           start=False, stop=True)
                        return ins
                    mm = P.add("tensor", vmm,
                               deps=xb + [ev_wv, psS_free[pp]], inc="pe",
                               pw=(2 * pp, 2 * pp + 1))
                    kv_mm_evs.append(mm)
                    ev = P.add("vector",
                               lambda e, t=t, pv=pv: e.tensor_copy(VT[:, ts(t, D)], pv[:]),
                               deps=[mm, last["dv"]], inc="dv",
                               pr=(2 * pp, 2 * pp + 1))
                    last["dv"] = ev
                    psS_free[pp] = ev
                kcons_prev[g] = kv_mm_evs[-1]

            ev_kv_done = kv_mm_evs[-1]

            ev_wq = P.dma("gpsimd", wsl(WA), w3(Wsrc["q"]), deps=[ev_kv_done])
            ev_wo = P.dma("gpsimd", wsl(WB), w3(Wsrc["o"]), deps=[ev_kv_done])

            ev_drel = None
            drel_cons = None
            xbar_prev = None
            qcons_prev = None
            psQ_free = None
            psPV_free = None
            psOP_free = None
            pt_free = None
            p_free = [None, None]
            otf_free = None
            otm_free = None
            ot2_free = None
            sofs = [(32, 48), (64, 80)]
            rofs = (0, 16)
            aofs = (96, 112)
            rs_free = [None, None]

            for g in range(8):
                exn = []
                for ti in range(2):
                    t = 2 * g + ti
                    ev = P.add("vector",
                               lambda e, t=t, ti=ti: e.tensor_scalar(
                                   XN[:, ts(ti, D)], X[:, ts(t, D)],
                                   STAT[:, C_R + t:C_R + t + 1], STAT[:, C_H + t:C_H + t + 1],
                                   ALU.mult, ALU.add),
                               deps=[ev_stats, last["dv"], xbar_prev], inc="dv")
                    last["dv"] = ev
                    exn.append(ev)
                o3 = XNT[:].rearrange("p (c x) -> p c x", x=256)
                xb = []
                for ti in range(2):
                    i3 = XN[:, ts(ti, D)].rearrange("p (c f) -> p c f", f=128)
                    xb.append(P.dmat(o3[:, :, ts(ti, 128)], i3, deps=exn + [qcons_prev]))
                xbar_prev = xb[-1]

                qmm_evs = []
                for oc in range(8):
                    def qmm(e, oc=oc):
                        ins = None
                        for ic in range(8):
                            ins = e.matmul(psQ[:, 0:256],
                                           WA[:, ic * D + oc * 128: ic * D + (oc + 1) * 128],
                                           XNT[:, ts(ic, 256)],
                                           start=(ic == 0), stop=(ic == 7))
                        return ins
                    mm = P.add("tensor", qmm, deps=xb + [ev_wq, psQ_free], inc="pe",
                               pw=(7,))
                    qmm_evs.append(mm)
                    if a_mode:
                        def qev(e, oc=oc):
                            return e.tensor_scalar(QT[:, ts(oc, 256)], psQ[:, 0:256],
                                                   BQ[:, oc:oc + 1], 0.125, ALU.add, ALU.mult)
                    else:
                        def qev(e, oc=oc):
                            return e.tensor_single_scalar(QT[:, ts(oc, 256)], psQ[:, 0:256],
                                                          0.125, ALU.mult)
                    ev = P.add("vector", qev, deps=[mm, last["dv"]], inc="dv", pr=(7,))
                    last["dv"] = ev
                    psQ_free = ev
                ev_qdone = psQ_free
                qcons_prev = qmm_evs[-1]

                for ti in range(2):
                    a = 2 * g + ti
                    par = a % 2
                    sA, sB = sofs[par]
                    rcol = rofs[par]
                    acol = aofs[par]
                    exp_evs = []
                    sc_evs = []
                    pvq_evs = []

                    for h in range(16):
                        pb = h % 2
                        base = 64 * (h % 2)
                        ch = h // 2
                        lhs_q = QT[base:base + 64, ch * 256 + ti * 128: ch * 256 + (ti + 1) * 128]

                        if a_mode:
                            if a < 8:
                                ev_drel = P.dma("gpsimd", DREL[:], btrel_in[h],
                                                deps=[drel_cons])
                                ev_dsub = P.add("vector",
                                                lambda e, h=h: e.tensor_single_scalar(
                                                    DREL[:], DREL[:], CBREL[:, h:h + 1], ALU.subtract),
                                                deps=[ev_drel, last["dv"]], inc="dv")
                                last["dv"] = ev_dsub
                            halves = []
                            for hf in range(2):
                                pps = psS[hf]

                                def smm(e, lhs_q=lhs_q, hf=hf, base=base, ch=ch, pps=pps):
                                    e.matmul(pps[:, 0:512], lhs_q,
                                             KT[base:base + 64, ch * SEQ + hf * 1024: ch * SEQ + hf * 1024 + 512],
                                             start=True, stop=True)
                                    return e.matmul(pps[:, 512:1024], lhs_q,
                                                    KT[base:base + 64, ch * SEQ + hf * 1024 + 512: ch * SEQ + (hf + 1) * 1024],
                                                    start=True, stop=True)
                                dep_exp = exp_evs[-2] if len(exp_evs) >= 2 else None
                                mm = P.add("tensor", smm,
                                           deps=[ev_qdone, dep_exp], inc="pe",
                                           pw=(2 * hf, 2 * hf + 1))
                                add_ev = mm
                                if a < 8:
                                    for b in range(8 * hf, 8 * hf + 8):
                                        s = b - a
                                        if s >= 8:
                                            def dadd(e, b=b, s=s, hf=hf, pps=pps):
                                                return e.tensor_add(
                                                    pps[:, (b % 8) * 128:(b % 8 + 1) * 128],
                                                    pps[:, (b % 8) * 128:(b % 8 + 1) * 128],
                                                    DREL[:, (s - 8) * 128:(s - 7) * 128])
                                            add_ev = P.add("vector", dadd,
                                                           deps=[mm, ev_dsub, last["dv"]], inc="dv",
                                                           pw=(2 * hf + (b % 8) // 4,),
                                                           pr=(2 * hf + (b % 8) // 4,))
                                            last["dv"] = add_ev
                                ex = P.add("scalar",
                                           lambda e, hf=hf, pb=pb, h=h, pps=pps, sc=(sA if hf == 0 else sB):
                                           e.activation(PB[pb][:, hf * 1024:(hf + 1) * 1024], pps[:],
                                                        ACT.Exp,
                                                        accum_out=RS[:, sc + h:sc + h + 1]),
                                           deps=[add_ev, last["ac"], p_free[pb], rs_free[par]], inc="ac",
                                           pr=(2 * hf, 2 * hf + 1))
                                last["ac"] = ex
                                exp_evs.append(ex)
                                halves.append(ex)
                            if a < 8:
                                drel_cons = add_ev if a < 8 else None
                            i3 = PB[pb][:].rearrange("p (c f) -> p c f", f=128)
                            t3 = PT[:].rearrange("p (c f) -> p c f", f=128)
                            xt = P.dmat(t3, i3, deps=halves + [pt_free])
                            p_free[pb] = xt
                            nkc = 16
                            kbase = 0
                        else:
                            nkc = min(a, 4) + 1
                            kbase = max(0, a - 4)
                            pps = psS[pb]
                            wid = nkc * 128

                            def smm(e, lhs_q=lhs_q, base=base, ch=ch, a=a, nkc=nkc, kbase=kbase, pps=pps):
                                ins = None
                                if nkc > 1:
                                    ins = e.matmul(pps[:, 0:(nkc - 1) * 128], lhs_q,
                                                   KT[base:base + 64,
                                                      ch * SEQ + kbase * 128: ch * SEQ + a * 128],
                                                   start=True, stop=True)
                                ins = e.matmul(pps[:, (nkc - 1) * 128: nkc * 128], lhs_q,
                                               KT[base:base + 64, ch * SEQ + a * 128: ch * SEQ + (a + 1) * 128],
                                               start=True, stop=True)
                                return ins
                            dep_exp = exp_evs[-2] if len(exp_evs) >= 2 else None
                            mm = P.add("tensor", smm, deps=[ev_qdone, dep_exp], inc="pe",
                                       pw=(2 * pb, 2 * pb + 1))
                            def mdiag(e, nkc=nkc, pps=pps):
                                return e.tensor_add(pps[:, (nkc - 1) * 128: nkc * 128],
                                                    pps[:, (nkc - 1) * 128: nkc * 128],
                                                    MASKS[:, 0:128])
                            mev = P.add("vector", mdiag, deps=[mm, last["dv"]], inc="dv",
                                        pw=(2 * pb + (nkc - 1) // 4,), pr=(2 * pb + (nkc - 1) // 4,))
                            last["dv"] = mev
                            if a >= 4:
                                def mlow(e, pps=pps):
                                    return e.tensor_add(pps[:, 0:128], pps[:, 0:128],
                                                        MASKS[:, 128:256])
                                mev = P.add("vector", mlow, deps=[mm, last["dv"]], inc="dv",
                                            pw=(2 * pb,), pr=(2 * pb,))
                                last["dv"] = mev
                            ex = P.add("scalar",
                                       lambda e, pb=pb, h=h, wid=wid, sc=sA, pps=pps:
                                       e.activation(PB[pb][:, 0:wid], pps[:, 0:wid], ACT.Exp,
                                                    accum_out=RS[:, sc + h:sc + h + 1]),
                                       deps=[mev, last["ac"], p_free[pb], rs_free[par]], inc="ac",
                                       pr=(2 * pb, 2 * pb + 1))
                            last["ac"] = ex
                            exp_evs.append(ex)
                            i3 = PB[pb][:, 0:wid].rearrange("p (c f) -> p c f", f=128)
                            t3 = PT[:, 0:wid].rearrange("p (c f) -> p c f", f=128)
                            xt = P.dmat(t3, i3, deps=[ex, pt_free])
                            p_free[pb] = xt

                        j = h % 4

                        def pvmm(e, h=h, j=j, nkc=nkc, kbase=kbase):
                            ins = None
                            for kk in range(nkc):
                                kc = kbase + kk
                                ins = e.matmul(psPV[0:64, j * 128:(j + 1) * 128],
                                               VT[:, kc * D + h * 64: kc * D + (h + 1) * 64],
                                               PT[:, ts(kk, 128)],
                                               start=(kk == 0), stop=(kk == nkc - 1))
                            return ins
                        mmpv = P.add("tensor", pvmm, deps=[xt, psPV_free], inc="pe",
                                     pw=(4,))
                        pt_free = mmpv
                        pvq_evs.append(mmpv)
                        if j == 3:
                            for jj in range(4):
                                hh = h - 3 + jj

                                def pvev(e, jj=jj, hh=hh):
                                    return e.tensor_copy(
                                        OTF[64 * (hh % 2):64 * (hh % 2) + 64, ts(hh // 2, 128)],
                                        psPV[0:64, jj * 128:(jj + 1) * 128])
                                ev = P.add("vector", pvev,
                                           deps=[mmpv, last["dv"], otf_free], inc="dv",
                                           pr=(4,))
                                last["dv"] = ev
                                if jj == 3:
                                    psPV_free = ev
                                sc_evs.append(ev)

                    if a_mode:
                        ev_rs = cdv(lambda e, sA=sA, sB=sB, rcol=rcol: e.tensor_add(
                            RS[:, rcol:rcol + 16], RS[:, sA:sA + 16], RS[:, sB:sB + 16]),
                            deps=exp_evs)
                        ev_al = cdv(lambda e, rcol=rcol, acol=acol: e.reciprocal(
                            RS[:, acol:acol + 16], RS[:, rcol:rcol + 16]))
                        rs_free[par] = ev_rs
                    else:
                        ev_al = cdv(lambda e, sA=sA, acol=acol: e.reciprocal(
                            RS[:, acol:acol + 16], RS[:, sA:sA + 16]), deps=exp_evs)
                        rs_free[par] = ev_al

                    i3 = OTF[:].rearrange("p (c f) -> p c f", f=128)
                    m3 = OTM[:].rearrange("p (c f) -> p c f", f=128)
                    xt1 = P.dmat(m3, i3, deps=sc_evs + [otm_free])
                    otf_free = xt1
                    scl = []
                    for h in range(16):
                        ev = P.add("vector",
                                   lambda e, h=h, acol=acol: e.tensor_single_scalar(
                                       OTM[:, ts(h, 64)], OTM[:, ts(h, 64)],
                                       RS[:, acol + h:acol + h + 1], ALU.mult),
                                   deps=[xt1, ev_al, last["dv"]], inc="dv")
                        last["dv"] = ev
                        scl.append(ev)
                    m3b = OTM[:].rearrange("p (c f) -> p c f", f=128)
                    o3b = OT2[:].rearrange("p (c f) -> p c f", f=128)
                    xt2 = P.dmat(o3b, m3b, deps=scl + [ot2_free])
                    otm_free = xt2

                    def opmm(e, a_mode=a_mode):
                        laststop = not a_mode
                        ins = None
                        for dc in range(8):
                            st_ = laststop and (dc == 7)
                            e.matmul(psOP[:, 0:512], OT2[:, ts(dc, 128)],
                                     WB[:, dc * D: dc * D + 512],
                                     start=(dc == 0), stop=st_)
                            ins = e.matmul(psOP[:, 512:1024], OT2[:, ts(dc, 128)],
                                           WB[:, dc * D + 512: (dc + 1) * D],
                                           start=(dc == 0), stop=st_)
                        if a_mode:
                            e.matmul(psOP[:, 0:512], ONES[0:1, 0:128], BO[0:1, 0:512],
                                     start=False, stop=True)
                            ins = e.matmul(psOP[:, 512:1024], ONES[0:1, 0:128], BO[0:1, 512:1024],
                                           start=False, stop=True)
                        return ins
                    mmop = P.add("tensor", opmm, deps=[xt2, ev_wo, psOP_free], inc="pe",
                                 pw=(5, 6))
                    ot2_free = mmop
                    ev = P.add("vector",
                               lambda e, a=a: e.scalar_tensor_tensor(
                                   X[:, ts(a, D)], psOP[:], 1.0, X[:, ts(a, D)],
                                   ALU.mult, ALU.add),
                               deps=[mmop, last["dv"]], inc="dv", pr=(5, 6))
                    last["dv"] = ev
                    psOP_free = ev
        return P.barrier()

    def ffn_phase(bar, seq, uid=[0]):
        uid[0] += 1
        pfx = "F%d_" % uid[0]
        with contextlib.ExitStack() as fsk:
            fsb = lambda n, s, d: fsk.enter_context(nc.sbuf_tensor(pfx + n, s, d))
            TG = fsb("TG", [128, 8 * INNER], FP8)
            TU = fsb("TU", [128, 8 * INNER], FP8)
            TD = fsb("TD", [128, 32 * D], FP8)
            HT = fsb("HT", [128, INNER], F32)
            SG = fsb("SG", [128, 512], F32)
            NTOK = [fsb("NTOK0", [128, D], BF16)] * 2
            NTT = [fsb("NTT0", [128, D], BF16)] * 2
            N2 = fsb("N2", [128, INNER], BF16)
            N2T = fsb("N2T", [128, INNER], BF16)
            ZT = fsb("ZT", [128, 512], F32)

            ev_tg = P.dma("sync", TG[:], tg_d, deps=[bar])
            ev_tu = P.dma("sync", TU[:], tu_d, deps=[bar])
            ev_td = P.dma("sync", TD[:], td_d, deps=[bar])

            ev_stats = stats_phase(bar, need_minmax=True, dump=HT[:, 0:1024])
            cdv(lambda e: e.tensor_sub(RS[:, 64:80], STAT[:, C_MX:C_MX + 16], STAT[:, C_M:C_M + 16]),
                deps=[ev_stats])
            cdv(lambda e: e.tensor_sub(RS[:, 80:96], STAT[:, C_M:C_M + 16], STAT[:, C_MN:C_MN + 16]))
            cdv(lambda e: e.tensor_max(RS[:, 64:80], RS[:, 64:80], RS[:, 80:96]))
            cdv(lambda e: e.tensor_mul(RS[:, 64:80], RS[:, 64:80], STAT[:, C_R:C_R + 16]))
            cdv(lambda e: e.tensor_single_scalar(RS[:, 64:80], RS[:, 64:80], 1e-5, ALU.max))
            cdv(lambda e: e.reciprocal(RS[:, 80:96], RS[:, 64:80]))
            cdv(lambda e: e.tensor_mul(STAT[:, C_T:C_T + 16], STAT[:, C_R:C_R + 16], RS[:, 80:96]))
            cdv(lambda e: e.tensor_single_scalar(STAT[:, C_T:C_T + 16], STAT[:, C_T:C_T + 16],
                                                 127.0, ALU.mult))
            cdv(lambda e: e.scalar_tensor_tensor(STAT[:, C_H:C_H + 16], STAT[:, C_M:C_M + 16], -1.0,
                                                 STAT[:, C_T:C_T + 16], ALU.mult, ALU.mult))
            cdv(lambda e: e.tensor_scalar(RS[:, 96:112], RS[:, 64:80], MW[:, 0:1], 1.0 / 127.0,
                                          ALU.mult, ALU.mult))
            ev_drv = cdv(lambda e: e.tensor_scalar(RS[:, 112:128], RS[:, 64:80], MW[:, 1:2], 1.0 / 127.0,
                                                   ALU.mult, ALU.mult))

            ntok_free = [None, None]
            ntt_free = [None, None]
            gu_free = {0: None, 1: None}
            n2_free = None
            n2t_free = None
            sg_free = None
            ht_free = None
            psOP_free = None
            amax_cols = [(80, 88), (88, 96)]
            for t in range(NT):
                tp = 0
                e2 = None
                for hf in range(2):
                    e1 = P.add("vector",
                               lambda e, t=t, hf=hf: e.tensor_scalar(
                                   ZT[:], X[:, t * D + hf * 512: t * D + (hf + 1) * 512],
                                   STAT[:, C_T + t:C_T + t + 1],
                                   STAT[:, C_H + t:C_H + t + 1],
                                   ALU.mult, ALU.add),
                               deps=[ev_drv, last["dv"], ntok_free[tp] if hf == 0 else None],
                               inc="dv")
                    last["dv"] = e1
                    e2 = P.add("vector",
                               lambda e, hf=hf: e.tensor_scalar(
                                   NTOK[0][:, hf * 512:(hf + 1) * 512], ZT[:], CAP, CAP,
                                   ALU.add, ALU.subtract),
                               deps=[e1], inc="dv")
                    last["dv"] = e2
                i3 = NTOK[0][:].rearrange("p (c f) -> p c f", f=128)
                o3 = NTT[0][:].rearrange("p (c f) -> p c f", f=128)
                xb = P.dmat(o3, i3, deps=[e2, ntt_free[tp]])
                ntok_free[tp] = xb

                silu_evs = []
                hstt_evs = []
                gp_amax = []
                for os_ in range(8):
                    pp = os_ % 2
                    pg, pu = psS[pp][:, 0:512], psS[pp][:, 512:1024]

                    def gumm(e, tp=tp, os_=os_, pg=pg, pu=pu):
                        ins = None
                        for ic in range(8):
                            e.matmul(pg, NTT[0][:, ts(ic, 128)],
                                     TG[:, (os_ * 8 + ic) * 512:(os_ * 8 + ic + 1) * 512],
                                     start=(ic == 0), stop=(ic == 7))
                            ins = e.matmul(pu, NTT[0][:, ts(ic, 128)],
                                           TU[:, (os_ * 8 + ic) * 512:(os_ * 8 + ic + 1) * 512],
                                           start=(ic == 0), stop=(ic == 7))
                        return ins
                    mm = P.add("tensor", gumm,
                               deps=[xb, ev_tg, ev_tu, gu_free[pp]], inc="pe",
                               pw=(2 * pp, 2 * pp + 1))
                    sev = P.add("scalar",
                                lambda e, t=t, pg=pg: e.activation(SG[:], pg, ACT.Silu,
                                                                   scale=RS[:, 96 + t:97 + t]),
                                deps=[mm, last["ac"], sg_free], inc="ac", pr=(2 * pp,))
                    last["ac"] = sev
                    silu_evs.append(sev)
                    hev = P.add("vector",
                                lambda e, t=t, os_=os_, pu=pu: e.scalar_tensor_tensor(
                                    HT[:, ts(os_, 512)], pu, RS[:, 112 + t:113 + t], SG[:],
                                    ALU.mult, ALU.mult),
                                deps=[sev, mm, last["dv"], ht_free], inc="dv",
                                pr=(2 * pp + 1,))
                    last["dv"] = hev
                    sg_free = hev
                    hstt_evs.append(hev)
                    gu_free[pp] = hev
                    ntt_free[tp] = mm
                e3 = cdv(lambda e, tp=tp: e.tensor_reduce(
                    RS[:, 48 + tp:49 + tp], HT[:], AXX, ALU.max,
                    apply_absolute_value=True), deps=hstt_evs)
                e4 = cdv(lambda e, tp=tp: e.tensor_single_scalar(
                    RS[:, 48 + tp:49 + tp], RS[:, 48 + tp:49 + tp], 1e-5, ALU.max))
                e5 = cdv(lambda e, tp=tp: e.reciprocal(RS[:, 50 + tp:51 + tp],
                                                       RS[:, 48 + tp:49 + tp]))
                e6 = cdv(lambda e, tp=tp: e.tensor_single_scalar(
                    RS[:, 50 + tp:51 + tp], RS[:, 50 + tp:51 + tp], 127.0, ALU.mult))
                e7 = cdv(lambda e, tp=tp: e.tensor_scalar(
                    RS[:, 52 + tp:53 + tp], RS[:, 48 + tp:49 + tp], MW[:, 2:3], 1.0 / 127.0,
                    ALU.mult, ALU.mult))
                q_evs = []
                prev = e7
                for os_ in range(8):
                    za = P.add("vector",
                               lambda e, os_=os_, tp=tp: e.tensor_scalar(
                                   SG[:], HT[:, ts(os_, 512)], RS[:, 50 + tp:51 + tp], CAP,
                                   ALU.mult, ALU.add),
                               deps=[prev, hstt_evs[os_], last["dv"], n2_free], inc="dv")
                    last["dv"] = za
                    zb = P.add("vector",
                               lambda e, os_=os_: e.tensor_scalar(
                                   N2[:, ts(os_, 512)], SG[:], CAP, None, ALU.subtract),
                               deps=[za], inc="dv")
                    last["dv"] = zb
                    prev = zb
                    q_evs.append(zb)
                sg_free = q_evs[-1]
                ht_free = q_evs[-1]
                i3 = N2[:].rearrange("p (c f) -> p c f", f=128)
                o3 = N2T[:].rearrange("p (c f) -> p c f", f=128)
                xb2 = P.dmat(o3, i3, deps=q_evs + [n2t_free])
                n2_free = xb2

                def dmm(e):
                    ins = None
                    for ic2 in range(32):
                        e.matmul(psOP[:, 0:512], N2T[:, ts(ic2, 128)],
                                 TD[:, ic2 * D: ic2 * D + 512],
                                 start=(ic2 == 0), stop=(ic2 == 31))
                        ins = e.matmul(psOP[:, 512:1024], N2T[:, ts(ic2, 128)],
                                       TD[:, ic2 * D + 512: (ic2 + 1) * D],
                                       start=(ic2 == 0), stop=(ic2 == 31))
                    return ins
                mmd = P.add("tensor", dmm, deps=[xb2, ev_td, psOP_free], inc="pe",
                            pw=(5, 6))
                n2t_free = mmd
                ev_x0 = P.add("vector",
                              lambda e, t=t, tp=tp: e.scalar_tensor_tensor(
                                  X[:, t * D: t * D + 512], psOP[:, 0:512],
                                  RS[:, 52 + tp:53 + tp], X[:, t * D: t * D + 512],
                                  ALU.mult, ALU.add),
                              deps=[mmd, last["dv"]], inc="dv", pr=(5,))
                last["dv"] = ev_x0
                ev_x1 = P.add("vector",
                              lambda e, t=t, tp=tp: e.scalar_tensor_tensor(
                                  X[:, t * D + 512: (t + 1) * D], psOP[:, 512:1024],
                                  RS[:, 52 + tp:53 + tp], X[:, t * D + 512: (t + 1) * D],
                                  ALU.mult, ALU.add),
                              deps=[mmd, ev_x0], inc="dv", pr=(6,))
                last["dv"] = ev_x1
                psOP_free = ev_x1
        return P.barrier()

    store_evs = []
    for s in range(nseq):
        xr = x_in[s].rearrange("(t p) d -> p t d", p=128)
        for i in range(4):
            P.dma("gpsimd", X[:, i * 4096:(i + 1) * 4096], xr[:, 4 * i:4 * i + 4, :],
                  deps=[bar])
        bar = P.barrier()
        for kind in sublayers:
            if kind == "ffn":
                bar = ffn_phase(bar, s)
            else:
                bar = attn_phase(bar, kind, s)
        yr = y_out[s].rearrange("(t p) d -> p t d", p=128)
        for i in range(4):
            store_evs.append(
                P.dma("gpsimd", yr[:, 4 * i:4 * i + 4, :], X[:, i * 4096:(i + 1) * 4096],
                      deps=[bar]))
        bar = P.barrier()

    P.add("sync", lambda e: e.nop(), deps=store_evs)

    with nc.Block() as block:
        P.emit(block)
    stack.close()
    return nc


def _stage(inputs):
    f32 = np.float32
    g = {}
    g["aWq"] = np.ascontiguousarray(inputs["L0_attn_a_Wq"], dtype=f32)
    g["aWk"] = np.ascontiguousarray(inputs["L0_attn_a_Wk"], dtype=f32)
    g["aWv"] = np.ascontiguousarray(inputs["L0_attn_a_Wv"], dtype=f32)
    g["aWo"] = np.ascontiguousarray(inputs["L0_attn_a_Wo"], dtype=f32)
    g["bWq"] = np.ascontiguousarray(inputs["L0_attn_b_Wq"].T, dtype=f32)
    g["bWk"] = np.ascontiguousarray(inputs["L0_attn_b_Wk"].T, dtype=f32)
    g["bWv"] = np.ascontiguousarray(inputs["L0_attn_b_Wv"].T, dtype=f32)
    g["bWo"] = np.ascontiguousarray(inputs["L0_attn_b_Wo"].T, dtype=f32)
    g["abq"] = np.ascontiguousarray(np.asarray(inputs["L0_attn_a_bq"], dtype=f32).reshape(8, 128).T)
    g["abk"] = np.ascontiguousarray(np.asarray(inputs["L0_attn_a_bk"], dtype=f32).reshape(8, 128).T)
    g["abo"] = np.ascontiguousarray(np.asarray(inputs["L0_attn_a_bo"], dtype=f32).reshape(1, D))
    g["abv"] = np.ascontiguousarray(np.asarray(inputs["L0_attn_a_bv"], dtype=f32).reshape(1, D))
    rel = np.asarray(inputs["L0_attn_a_rel_table"], dtype=f32)
    idx = np.clip(np.arange(128)[:, None] - np.arange(1024)[None, :] + 1023, 0, 1022)
    bt = rel[idx, :]
    g["btrel"] = np.ascontiguousarray(np.transpose(bt, (2, 0, 1)))
    g["cbrel"] = np.ascontiguousarray(np.broadcast_to(rel[1022], (128, H)))
    di = np.arange(128)[:, None]
    dj = np.arange(128)[None, :]
    m_up = np.where(dj > di, -1e9, 0.0).astype(f32)
    m_low = np.where(dj <= di, -1e9, 0.0).astype(f32)
    g["masks"] = np.ascontiguousarray(np.concatenate([m_up, m_low], axis=1))
    g["wgT"] = np.ascontiguousarray(inputs["L0_ffn_w_gate"].T, dtype=f32)
    g["wuT"] = np.ascontiguousarray(inputs["L0_ffn_w_up"].T, dtype=f32)
    g["wdT"] = np.ascontiguousarray(inputs["L0_ffn_w_down"].T, dtype=f32)
    return g


def kernel(**inputs):
    x = np.ascontiguousarray(np.asarray(inputs["L0_attn_a_x"], dtype=np.float32))
    g = _stage(inputs)
    nc = build_program()
    in_maps = []
    for c in range(8):
        m = dict(g)
        m["x"] = np.ascontiguousarray(x[2 * c: 2 * c + 2])
        in_maps.append(m)
    res = run_bass_kernel_spmd(nc, in_maps, core_ids=list(range(8)))
    out = np.concatenate([r["y"] for r in res.results], axis=0)
    return out.astype(np.float32)
